# version 09
# baseline (speedup 1.0000x reference)
import contextlib
import numpy as np
import concourse.bass as bass
import concourse.mybir as mybir
import concourse.bass_utils as _bu
from concourse.bass_utils import run_bass_kernel_spmd

if not getattr(_bu, "_ldw_opt_patched", False):
    _orig_run_command = _bu.run_command
    def _run_command_ldw(cmd, *a, **kw):
        try:
            cmd = ["--enable-ldw-opt=true" if c == "--enable-ldw-opt=false" else c for c in cmd]
        except TypeError:
            pass
        return _orig_run_command(cmd, *a, **kw)
    _bu.run_command = _run_command_ldw
    _bu._ldw_opt_patched = True

F32 = mybir.dt.float32
ALU = mybir.AluOpType
AF = mybir.ActivationFunctionType
AX = mybir.AxisListType.X

D = 1024
KC = 8
IDIM = 2816
T = 2048
NT = 16
ISLICES = [(i * 256, 256) for i in range(IDIM // 256)]

QPOLY_SCALE = float(1.0 / (4.0 * np.sqrt(2.0)))
QPOLY_BIAS = float(np.sqrt(2.0) / 2.0)


class Plan:
    ENGINES = ("sync", "tensor", "vector", "scalar", "gpsimd")

    def __init__(self, nc):
        self.nc = nc
        self.ops = {e: [] for e in self.ENGINES}
        self.sem_handles = {}
        self.counts = {}
        self.last_wait = {}

    def add_sem(self, name, handle):
        self.sem_handles[name] = handle
        self.counts[name] = 0

    def op(self, engine, fn, inc=None, n=1):
        if inc is not None:
            self.counts[inc] += n
        sem = self.sem_handles[inc] if inc is not None else None

        def run(eng, fn=fn, sem=sem, n=n):
            instr = fn(eng)
            if sem is not None:
                instr.then_inc(sem, n)
        self.ops[engine].append(run)
        return self.counts.get(inc)

    def wait(self, engine, sem, value=None):
        value = self.counts[sem] if value is None else value
        if value <= 0:
            return
        key = (engine, sem)
        if self.last_wait.get(key, 0) >= value:
            return
        self.last_wait[key] = value
        h = self.sem_handles[sem]
        self.ops[engine].append(lambda eng, h=h, v=value: eng.wait_ge(h, v))

    def barrier(self, engines=None, sems=None):
        engines = engines or self.ENGINES
        sems = sems or list(self.sem_handles)
        for e in engines:
            for s in sems:
                self.wait(e, s)

    def materialize(self, block):
        for ename in self.ENGINES:
            ops = self.ops[ename]
            fn = {"sync": block.sync, "tensor": block.tensor, "vector": block.vector,
                  "scalar": block.scalar, "gpsimd": block.gpsimd}[ename]

            def body(eng, ops=ops):
                for o in ops:
                    o(eng)
            fn(body)


def build_core_program(n_seq=2, phases=("f1", "attn", "f2"), debug_out=None):
    nc = bass.Bass("TRN2", target_bir_lowering=False)
    NTOK = n_seq * T

    x_d = nc.dram_tensor("x", [NTOK, D], F32, kind="ExternalInput").ap()
    wqT_d = nc.dram_tensor("wqT", [D, 256], F32, kind="ExternalInput").ap()
    wkT_d = nc.dram_tensor("wkT", [D, 256], F32, kind="ExternalInput").ap()
    wvT_d = nc.dram_tensor("wvT", [D, D], F32, kind="ExternalInput").ap()
    woT_d = nc.dram_tensor("woT", [D, D], F32, kind="ExternalInput").ap()
    wg_d = nc.dram_tensor("wg", [IDIM, D], F32, kind="ExternalInput").ap()
    wu_d = nc.dram_tensor("wu", [IDIM, D], F32, kind="ExternalInput").ap()
    wd_d = nc.dram_tensor("wd", [D, IDIM], F32, kind="ExternalInput").ap()
    ident_d = nc.dram_tensor("ident", [128, 128], F32, kind="ExternalInput").ap()
    triu_d = nc.dram_tensor("triu", [128, 128], F32, kind="ExternalInput").ap()
    out_d = nc.dram_tensor("out", [NTOK, D], F32, kind="ExternalOutput").ap()

    wgTq_d = nc.dram_tensor("wgTq", [D, IDIM], F32, kind="Internal").ap()
    wuTq_d = nc.dram_tensor("wuTq", [D, IDIM], F32, kind="Internal").ap()
    wdTq_d = nc.dram_tensor("wdTq", [IDIM, D], F32, kind="Internal").ap()
    zp_d = nc.dram_tensor("zp", [T, D], F32, kind="Internal").ap()

    stack = contextlib.ExitStack()
    sb = lambda name, shape: stack.enter_context(nc.sbuf_tensor(name, shape, F32))
    ps = lambda name: stack.enter_context(nc.psum_tensor(name, [128, 512], F32))

    z = sb("s_z", [128, NT, D])
    arenaA_t = sb("s_arenaA", [128, KC * T + 128])
    arenaA = arenaA_t[:]
    xnqT = arenaA[:, 0:KC * T].rearrange("p (c t) -> p c t", c=KC)
    v65 = arenaA[:, 0:NT * 520].rearrange("p (t h) -> p t h", h=520)
    k32 = arenaA[:, NT * 520:NT * 520 + 2 * T].rearrange("p (c t) -> p c t", c=2)
    q32 = arenaA[:, NT * 520 + 2 * T:NT * 520 + 4 * T].rearrange("p (c t) -> p c t", c=2)
    qst = arenaA[:, NT * 520:NT * 520 + T]

    watt = sb("s_watt", [128, 12288])
    def _fslot(s):
        base = s * 6144
        return (watt[:, base:base + KC * 256].rearrange("p (c t) -> p c t", c=KC),
                watt[:, base + KC * 256:base + 2 * KC * 256].rearrange("p (c t) -> p c t", c=KC),
                watt[:, base + 2 * KC * 256:base + 2 * KC * 256 + 2 * D].rearrange("p (c t) -> p c t", c=2))
    fslots = [_fslot(0), _fslot(1)]
    wqb = watt[:, 0:KC * 128].rearrange("p (c t) -> p c t", c=KC)
    wkb = watt[:, KC * 128:2 * KC * 128].rearrange("p (c t) -> p c t", c=KC)
    wvb = watt[:, 2 * KC * 128:2 * KC * 128 + KC * 512].rearrange("p (c t) -> p c t", c=KC)
    wob = watt[:, 2 * KC * 128 + KC * 512:2 * KC * 128 + KC * 512 + 4 * D].rearrange("p (c t) -> p c t", c=4)

    y_t = sb("s_y", [128, D])
    a_t = sb("s_a", [128, D])
    scr4 = sb("s_scr4", [128, 3 * D])
    u_t = a_t[:, 0:D]
    qA = scr4[:, 0:D]
    qB = scr4[:, D:2 * D]
    qq = scr4[:, 2 * D:3 * D]
    kst = scr4[:, 0:T]
    zpstage = scr4[:, 2 * D:3 * D]
    sg_t = sb("s_sg", [128, D])
    ystage = sg_t[:, 0:512]
    small = sb("s_small", [128, 16])
    bm_t = sb("s_bm", [128, 64])
    sp_t = sb("s_sp", [128, 64])
    rr_t = sb("s_rr", [128, 64])
    stats = sb("s_stats", [128, 64])
    yT_wide = sb("s_yTw", [128, KC, 128])
    yT = yT_wide[:, 0:4, :]
    hqT = yT_wide[:, 4:8, :]
    ident_s = sb("s_ident", [128, 128])
    triu_s = a_t[:, 0:128]
    aa = y_t[:, 0:256].rearrange("p (c t) -> p c t", c=2)
    bconst = small[:, 8:9]
    hconst = small[:, 9:10]

    SUMS, SSQ, MEAN, NMEAN, VAR, IV, DEN, RCP = range(8)
    sm = lambda i: small[:, i:i + 1]

    pb = [ps(f"p_b{i}") for i in range(8)]

    P = Plan(nc)
    for sname in ("ld", "gd", "tn", "ve", "sl"):
        P.add_sem(sname, stack.enter_context(nc.semaphore("sem_" + sname)))

    bank_cons = {i: [] for i in range(8)}
    fslot_cons = {0: [], 1: []}

    def bank_write_ready(bi):
        for (s, c) in bank_cons[bi]:
            P.wait("tensor", s, c)
        bank_cons[bi] = []

    def bank_consumed(bi, sem):
        bank_cons[bi].append((sem, P.counts[sem]))

    def batched_ln_stats():
        P.wait("scalar", "ve")
        for t in range(NT):
            P.op("scalar", lambda e, t=t: e.activation(
                a_t[:], z[:, t, :], AF.Copy, accum_out=stats[:, t:t + 1]), inc="sl")
        P.wait("vector", "sl")
        P.op("vector", lambda e: e.tensor_scalar(
            stats[:, 16:32], stats[:, 0:16], 1.0 / D, None, op0=ALU.mult), inc="ve")
        P.op("vector", lambda e: e.tensor_scalar(
            stats[:, 32:48], stats[:, 0:16], -1.0 / D, None, op0=ALU.mult), inc="ve")
        P.wait("scalar", "ve")
        for t in range(NT):
            P.op("scalar", lambda e, t=t: e.activation(
                a_t[:], z[:, t, :], AF.Square, bias=stats[:, 32 + t:33 + t], scale=1.0,
                accum_out=stats[:, t:t + 1]), inc="sl")
        P.wait("vector", "sl")
        P.op("vector", lambda e: e.tensor_scalar(
            stats[:, 0:16], stats[:, 0:16], 1.0 / D, 1e-5, op0=ALU.mult, op1=ALU.add), inc="ve")
        P.wait("vector", "ve")
        P.op("vector", lambda e: e.reciprocal(stats[:, 48:64], stats[:, 0:16]), inc="ve")
        P.wait("vector", "ve")
        P.wait("scalar", "ve")
        P.op("scalar", lambda e: e.activation(
            stats[:, 48:64], stats[:, 48:64], AF.Sqrt), inc="sl")

    def st_mean(t):
        return stats[:, 16 + t:17 + t]

    def st_iv(t):
        return stats[:, 48 + t:49 + t]

    def ln_stats(src_ap):
        P.wait("scalar", "ve")
        P.op("scalar", lambda e, s=src_ap: e.activation(
            a_t[:], s, AF.Copy, accum_out=sm(SUMS)), inc="sl")
        P.wait("vector", "sl")
        P.op("vector", lambda e: e.tensor_scalar(
            sm(MEAN), sm(SUMS), 1.0 / D, None, op0=ALU.mult), inc="ve")
        P.op("vector", lambda e: e.tensor_scalar(
            sm(NMEAN), sm(SUMS), -1.0 / D, None, op0=ALU.mult), inc="ve")
        P.wait("scalar", "ve")
        P.op("scalar", lambda e, s=src_ap: e.activation(
            a_t[:], s, AF.Square, bias=sm(NMEAN), scale=1.0, accum_out=sm(SSQ)), inc="sl")
        P.wait("vector", "sl")
        P.op("vector", lambda e: e.tensor_scalar(
            sm(VAR), sm(SSQ), 1.0 / D, 1e-5, op0=ALU.mult, op1=ALU.add), inc="ve")
        P.wait("vector", "ve")
        P.op("vector", lambda e: e.reciprocal(sm(RCP), sm(VAR)), inc="ve")
        P.wait("scalar", "ve")
        P.op("scalar", lambda e: e.activation(sm(IV), sm(RCP), AF.Sqrt), inc="sl")

    def quant_core(ysrc, width, dst_ap, scale_src, iv_ap=None):
        nb = width // 16
        P.op("vector", lambda e, y=ysrc, w=width: e.scalar_tensor_tensor(
            a_t[:, 0:w], y, -1.0, y, op0=ALU.mult, op1=ALU.max), inc="ve")
        P.wait("scalar", "ve")
        P.op("scalar", lambda e, y=ysrc, w=width: e.activation(
            sg_t[:, 0:w], y, AF.Sign), inc="sl")
        a3 = a_t[:, 0:width].rearrange("p (b k) -> p b k", k=16)
        P.op("vector", lambda e, a3=a3, nb=nb: e.tensor_reduce(
            bm_t[:, 0:nb], a3, axis=AX, op=ALU.max), inc="ve")
        P.wait("vector", "ve")
        if scale_src == "ln":
            P.wait("vector", "sl")
            P.op("vector", lambda e, nb=nb, iv_ap=iv_ap: e.tensor_scalar(
                rr_t[:, 0:nb], bm_t[:, 0:nb], iv_ap, 1.0 / 6.0,
                op0=ALU.mult, op1=ALU.mult), inc="ve")
            P.op("vector", lambda e, nb=nb: e.tensor_scalar(
                sp_t[:, 0:nb], rr_t[:, 0:nb], 1e-12, None, op0=ALU.max), inc="ve")
        else:
            P.op("vector", lambda e, nb=nb: e.tensor_scalar(
                sp_t[:, 0:nb], bm_t[:, 0:nb], 1.0 / 6.0, 1e-12,
                op0=ALU.mult, op1=ALU.max), inc="ve")
        P.wait("vector", "ve")
        P.op("vector", lambda e, nb=nb: e.reciprocal(rr_t[:, 0:nb], sp_t[:, 0:nb]), inc="ve")
        P.wait("vector", "ve")
        if scale_src == "ln":
            P.op("vector", lambda e, nb=nb, iv_ap=iv_ap: e.tensor_scalar(
                rr_t[:, 0:nb], rr_t[:, 0:nb], iv_ap, None, op0=ALU.mult), inc="ve")
        rb = rr_t[:, 0:nb].rearrange("p (b o) -> p b o", o=1).broadcast_to([128, nb, 16])
        u3 = u_t[:, 0:width].rearrange("p (b k) -> p b k", k=16)
        P.wait("vector", "ve")
        P.op("vector", lambda e, rb=rb, a3=a3, u3=u3: e.tensor_tensor(
            u3, rb, a3, op=ALU.mult), inc="ve")
        uu = u_t[:, 0:width]
        P.op("vector", lambda e, u=uu, w=width: e.tensor_scalar(
            qA[:, 0:w], u, 2.5, None, op0=ALU.is_gt), inc="ve")
        for thr in (3.5, 5.0, 5.0):
            P.op("vector", lambda e, u=uu, w=width, t=thr: e.scalar_tensor_tensor(
                qA[:, 0:w], u, t, qA[:, 0:w], op0=ALU.is_gt, op1=ALU.add), inc="ve")
        P.op("vector", lambda e, w=width: e.tensor_scalar(
            qA[:, 0:w], qA[:, 0:w], 2.0, None, op0=ALU.mult), inc="ve")
        for thr in (0.25, 0.75, 1.25, 1.75):
            P.op("vector", lambda e, u=uu, w=width, t=thr: e.scalar_tensor_tensor(
                qA[:, 0:w], u, t, qA[:, 0:w], op0=ALU.is_gt, op1=ALU.add), inc="ve")
        P.op("vector", lambda e, nb=nb: e.tensor_scalar(
            sp_t[:, 0:nb], sp_t[:, 0:nb], 0.5, None, op0=ALU.mult), inc="ve")
        sb3 = sp_t[:, 0:nb].rearrange("p (b o) -> p b o", o=1).broadcast_to([128, nb, 16])
        sg3 = sg_t[:, 0:width].rearrange("p (b k) -> p b k", k=16)
        P.wait("vector", "sl")
        P.wait("vector", "ve")
        P.op("vector", lambda e, sb3=sb3, sg3=sg3: e.tensor_tensor(
            sg3, sb3, sg3, op=ALU.mult), inc="ve")
        P.op("vector", lambda e, d=dst_ap, w=width: e.tensor_tensor(
            d, qA[:, 0:w], sg_t[:, 0:w], op=ALU.mult), inc="ve")

    def quant_ln_tile(src_ap, dst_ap, t):
        P.wait("vector", "sl")
        P.op("vector", lambda e, s=src_ap, t=t: e.tensor_scalar(
            y_t[:], s, st_mean(t), None, op0=ALU.subtract), inc="ve")
        quant_core(y_t[:], D, dst_ap, "ln", iv_ap=st_iv(t))

    def ln_full(src_ap, t):
        P.wait("vector", "sl")
        P.op("vector", lambda e, s=src_ap, t=t: e.tensor_scalar(
            y_t[:], s, st_mean(t), st_iv(t), op0=ALU.subtract, op1=ALU.mult), inc="ve")

    def transpose_to(src, n_chunks, dst3, dst_col0=0):
        for kc in range(n_chunks):
            bi = 4 + (kc % 2)
            bank_write_ready(bi)
            P.op("tensor", lambda e, s=src, kc=kc, bi=bi: e.transpose(
                pb[bi][:, 0:128], s[:, kc * 128:(kc + 1) * 128], ident_s[:]), inc="tn")
            P.wait("scalar", "tn")
            P.op("scalar", lambda e, kc=kc, bi=bi, d=dst3, c0=dst_col0: e.copy(
                d[:, kc, c0:c0 + 128], pb[bi][:, 0:128]), inc="sl")
            bank_consumed(bi, "sl")

    P.op("sync", lambda e: e.dma_start(out=ident_s[:], in_=ident_d), inc="ld", n=16)
    P.op("vector", lambda e: e.memset(bconst, QPOLY_BIAS), inc="ve")
    P.op("vector", lambda e: e.memset(hconst, 0.5), inc="ve")

    def preprocess_weight(w_dram, n_rows, k_dim, wT_dram):
        n_rt = n_rows // 128
        kcs = k_dim // 128
        w3 = w_dram.rearrange("(r p) k -> r p k", p=128)
        wT3 = wT_dram.rearrange("(c p) n -> p c n", p=128)
        strip = arenaA[:, 0:kcs * 128].rearrange("p (c t) -> p c t", c=kcs)
        ksl = [(0, D)] if k_dim == D else [(0, D), (D, D), (2 * D, k_dim - 2 * D)]
        for rt in range(n_rt):
            P.wait("scalar", "gd")
            for (k0, kw) in ksl:
                P.barrier(engines=("sync",), sems=("tn",))
                P.op("sync", lambda e, rt=rt, k0=k0, kw=kw: e.dma_start(
                    out=qq[:, 0:kw], in_=w3[rt, :, k0:k0 + kw]), inc="ld", n=16)
                P.wait("vector", "ld")
                P.wait("scalar", "ld")
                quant_core(qq[:, 0:kw], kw, qq[:, 0:kw], "plain")
                P.wait("tensor", "ve")
                for kc in range(kw // 128):
                    bi = 4 + (kc % 2)
                    bank_write_ready(bi)
                    P.op("tensor", lambda e, kc=kc, bi=bi: e.transpose(
                        pb[bi][:, 0:128], qq[:, kc * 128:(kc + 1) * 128], ident_s[:]), inc="tn")
                    P.wait("scalar", "tn")
                    P.op("scalar", lambda e, kc=kc, bi=bi, k0=k0, strip=strip: e.copy(
                        strip[:, k0 // 128 + kc, :], pb[bi][:, 0:128]), inc="sl")
                    bank_consumed(bi, "sl")
            P.wait("gpsimd", "sl")
            P.op("gpsimd", lambda e, rt=rt, wT3=wT3, strip=strip: e.dma_start(
                out=wT3[:, :, rt * 128:(rt + 1) * 128], in_=strip[:]), inc="gd", n=16)

    if "f1" in phases or "f2" in phases:
        preprocess_weight(wg_d, IDIM, D, wgTq_d)
        preprocess_weight(wu_d, IDIM, D, wuTq_d)
        preprocess_weight(wd_d, D, IDIM, wdTq_d)
    P.barrier(engines=("sync",), sems=("gd",))

    wgT3 = wgTq_d.rearrange("(c p) n -> p c n", p=128)
    wuT3 = wuTq_d.rearrange("(c p) n -> p c n", p=128)
    wdT3 = wdTq_d.rearrange("(c p) n -> p c n", p=128)
    x3 = x_d.rearrange("(n p) k -> n p k", p=128)
    out3 = out_d.rearrange("(n p) k -> n p k", p=128)
    zp3 = zp_d.rearrange("(n p) k -> n p k", p=128)

    def ffn(seq, first):
        if first:
            P.barrier(engines=("sync",), sems=("ve", "sl"))
            for t in range(NT):
                P.op("sync", lambda e, seq=seq, t=t: e.dma_start(
                    out=z[:, t, :], in_=x3[seq * NT + t]), inc="ld", n=16)
            P.wait("vector", "ld")
            P.wait("scalar", "ld")
        batched_ln_stats()

        def prep(t):
            P.wait("vector", "tn")
            quant_ln_tile(z[:, t, :], qq[:], t)
            P.wait("tensor", "ve")
            transpose_to(qq[:], KC, xnqT, t * 128)
        def emit_gateup(t, nw, slot):
            wgb, wub, _ = fslots[slot]
            gb, ub = (0, 1) if t % 2 == 0 else (2, 3)
            bank_write_ready(gb)
            bank_write_ready(ub)
            for kc in range(KC):
                P.op("tensor", lambda e, kc=kc, t=t, nw=nw, gb=gb, wgb=wgb: e.matmul(
                    pb[gb][:, 0:nw], xnqT[:, kc, t * 128:(t + 1) * 128],
                    wgb[:, kc, 0:nw], start=(kc == 0), stop=(kc == KC - 1)),
                    inc="tn" if kc == KC - 1 else None)
                P.op("tensor", lambda e, kc=kc, t=t, nw=nw, ub=ub, wub=wub: e.matmul(
                    pb[ub][:, 0:nw], xnqT[:, kc, t * 128:(t + 1) * 128],
                    wub[:, kc, 0:nw], start=(kc == 0), stop=(kc == KC - 1)),
                    inc="tn" if kc == KC - 1 else None)
            return P.counts["tn"]

        def emit_tail(t, nw, nkc4, gu_done, slot):
            _, _, wdb = fslots[slot]
            gb, ub = (0, 1) if t % 2 == 0 else (2, 3)
            P.wait("scalar", "tn", gu_done)
            P.wait("scalar", "ve")
            P.op("scalar", lambda e, nw=nw, gb=gb: e.activation(
                a_t[:, 0:nw], pb[gb][:, 0:nw], AF.Silu), inc="sl")
            bank_consumed(gb, "sl")
            P.wait("vector", "sl")
            P.wait("vector", "tn", gu_done)
            P.op("vector", lambda e, nw=nw, ub=ub: e.tensor_tensor(
                y_t[:, 0:nw], a_t[:, 0:nw], pb[ub][:, 0:nw], op=ALU.mult), inc="ve")
            bank_consumed(ub, "ve")
            quant_core(y_t[:, 0:nw], nw, qB[:, 0:nw], "plain")
            P.wait("tensor", "ve")
            for kc in range(nkc4):
                bi = 6 + (kc % 2)
                bank_write_ready(bi)
                P.op("tensor", lambda e, kc=kc, bi=bi: e.transpose(
                    pb[bi][:, 0:128], qB[:, kc * 128:(kc + 1) * 128], ident_s[:]), inc="tn")
                P.wait("scalar", "tn")
                P.op("scalar", lambda e, kc=kc, bi=bi: e.copy(
                    hqT[:, kc, :], pb[bi][:, 0:128]), inc="sl")
                bank_consumed(bi, "sl")
            bank_write_ready(4)
            bank_write_ready(5)
            P.wait("tensor", "sl")
            for kc in range(nkc4):
                P.op("tensor", lambda e, kc=kc, nkc4=nkc4, wdb=wdb: e.matmul(
                    pb[4][:, :], hqT[:, kc, :], wdb[:, kc, 0:512],
                    start=(kc == 0), stop=(kc == nkc4 - 1)),
                    inc="tn" if kc == nkc4 - 1 else None)
                P.op("tensor", lambda e, kc=kc, nkc4=nkc4, wdb=wdb: e.matmul(
                    pb[5][:, :], hqT[:, kc, :], wdb[:, kc, 512:1024],
                    start=(kc == 0), stop=(kc == nkc4 - 1)),
                    inc="tn" if kc == nkc4 - 1 else None)
            P.wait("vector", "tn")
            P.op("vector", lambda e, t=t: e.scalar_tensor_tensor(
                z[:, t, 0:512], pb[4][:, :], 0.5, z[:, t, 0:512],
                op0=ALU.mult, op1=ALU.add), inc="ve")
            bank_consumed(4, "ve")
            P.op("vector", lambda e, t=t: e.scalar_tensor_tensor(
                z[:, t, 512:1024], pb[5][:, :], 0.5, z[:, t, 512:1024],
                op0=ALU.mult, op1=ALU.add), inc="ve")
            bank_consumed(5, "ve")

        def emit_wload(isl):
            n0, nw = ISLICES[isl]
            slot = isl % 2
            wgb, wub, wdb = fslots[slot]
            nkc4 = nw // 128
            for (s, c) in fslot_cons[slot]:
                P.wait("sync", s, c)
            fslot_cons[slot] = []
            P.op("sync", lambda e, n0=n0, nw=nw, wgb=wgb: e.dma_start(
                out=wgb[:, :, 0:nw], in_=wgT3[:, :, n0:n0 + nw]), inc="ld", n=16)
            P.op("sync", lambda e, n0=n0, nw=nw, wub=wub: e.dma_start(
                out=wub[:, :, 0:nw], in_=wuT3[:, :, n0:n0 + nw]), inc="ld", n=16)
            P.op("sync", lambda e, n0=n0, nkc4=nkc4, wdb=wdb: e.dma_start(
                out=wdb[:, 0:nkc4, :], in_=wdT3[:, n0 // 128:n0 // 128 + nkc4, :]), inc="ld", n=16)
            return P.counts["ld"]

        for isl, (n0, nw) in enumerate(ISLICES):
            nkc4 = nw // 128
            slot = isl % 2
            if isl == 0:
                wload_done = emit_wload(0)
            nxt_load = emit_wload(isl + 1) if isl + 1 < len(ISLICES) else None
            P.wait("tensor", "ld", wload_done)
            if isl == 0:
                prep(0)
                P.wait("tensor", "sl")
                gu_done = emit_gateup(0, nw, slot)
                for t in range(NT):
                    nxt = None
                    if t + 1 < NT:
                        prep(t + 1)
                        P.wait("tensor", "sl")
                        nxt = emit_gateup(t + 1, nw, slot)
                    emit_tail(t, nw, nkc4, gu_done, slot)
                    gu_done = nxt
            else:
                gu_done = emit_gateup(0, nw, slot)
                for t in range(NT):
                    nxt = emit_gateup(t + 1, nw, slot) if t + 1 < NT else None
                    emit_tail(t, nw, nkc4, gu_done, slot)
                    gu_done = nxt
            fslot_cons[slot].append(("tn", P.counts["tn"]))
            wload_done = nxt_load

    def attention(seq):
        for hp in range(2):
            P.barrier()
            P.op("sync", lambda e, hp=hp: e.dma_start(
                out=wqb[:], in_=wqT_d.rearrange("(c p) n -> p c n", p=128)[:, :, hp * 128:(hp + 1) * 128]), inc="ld", n=16)
            P.op("sync", lambda e, hp=hp: e.dma_start(
                out=wkb[:], in_=wkT_d.rearrange("(c p) n -> p c n", p=128)[:, :, hp * 128:(hp + 1) * 128]), inc="ld", n=16)
            P.op("sync", lambda e, hp=hp: e.dma_start(
                out=wvb[:], in_=wvT_d.rearrange("(c p) n -> p c n", p=128)[:, :, hp * 512:(hp + 1) * 512]), inc="ld", n=16)
            P.op("sync", lambda e, hp=hp: e.dma_start(
                out=wob[:], in_=woT_d.rearrange("(c p) n -> p c n", p=128)[:, hp * 4:(hp + 1) * 4, :]), inc="ld", n=16)
            P.op("vector", lambda e: e.memset(
                v65[:].rearrange("p t (h u) -> p t h u", u=65)[:, :, :, 64:65], 1.0), inc="ve")
            P.wait("tensor", "ld")
            if hp == 0:
                batched_ln_stats()
            for t in range(NT):
                ln_full(z[:, t, :], t)
                P.wait("tensor", "ve")
                P.wait("tensor", "sl")
                transpose_to(y_t[:], KC, yT_wide, 0)
                P.wait("tensor", "sl")
                bank_write_ready(6)
                for kc in range(KC):
                    P.op("tensor", lambda e, kc=kc: e.matmul(
                        pb[6][:, 0:128], wkb[:, kc, :], yT_wide[:, kc, :],
                        start=(kc == 0), stop=(kc == KC - 1)),
                        inc="tn" if kc == KC - 1 else None)
                P.wait("scalar", "tn")
                P.op("scalar", lambda e, t=t: e.copy(
                    kst[:, t * 128:(t + 1) * 128], pb[6][:, 0:128]), inc="sl")
                bank_consumed(6, "sl")
                bank_write_ready(6)
                for kc in range(KC):
                    P.op("tensor", lambda e, kc=kc: e.matmul(
                        pb[6][:, 0:128], wqb[:, kc, :], yT_wide[:, kc, :],
                        start=(kc == 0), stop=(kc == KC - 1)),
                        inc="tn" if kc == KC - 1 else None)
                P.wait("scalar", "tn")
                P.op("scalar", lambda e, t=t: e.copy(
                    qst[:, t * 128:(t + 1) * 128], pb[6][:, 0:128]), inc="sl")
                bank_consumed(6, "sl")
                bank_write_ready(7)
                for kc in range(KC):
                    P.op("tensor", lambda e, kc=kc: e.matmul(
                        pb[7][:, :], yT_wide[:, kc, :], wvb[:, kc, :],
                        start=(kc == 0), stop=(kc == KC - 1)),
                        inc="tn" if kc == KC - 1 else None)
                P.wait("scalar", "tn")
                v65v = v65[:, t, :].rearrange("p (h u) -> p h u", u=65)
                P.op("scalar", lambda e, v65v=v65v: e.copy(
                    v65v[:, :, 0:64], pb[7][:, :].rearrange("p (h u) -> p h u", u=64)), inc="sl")
                bank_consumed(7, "sl")
            P.wait("sync", "sl")
            for h in range(8):
                g, c = h % 4, h // 4
                P.op("sync", lambda e, h=h, g=g, c=c: e.dma_start(
                    out=q32[32 * g:32 * g + 16, c, :], in_=qst[h * 16:(h + 1) * 16, :]), inc="ld", n=16)
            P.wait("sync", "ld")
            for h in range(8):
                g, c = h % 4, h // 4
                P.op("sync", lambda e, h=h, g=g, c=c: e.dma_start(
                    out=k32[32 * g:32 * g + 16, c, :], in_=kst[h * 16:(h + 1) * 16, :]), inc="ld", n=16)
            P.wait("sync", "sl")
            P.wait("sync", "ve")
            P.op("sync", lambda e: e.dma_start(out=triu_s, in_=triu_d), inc="ld", n=16)
            P.wait("tensor", "ld")
            P.wait("vector", "ld")
            aa8 = y_t[:].rearrange("p (c t) -> p c t", c=8)

            def emit_wave_qk(g, c, ti, w):
                tjs = list(range(4 * w, min(4 * w + 4, ti + 1)))
                sb_ = w % 2
                bank_write_ready(sb_)
                for k, tj in enumerate(tjs):
                    P.op("tensor", lambda e, g=g, c=c, ti=ti, tj=tj, k=k, sb_=sb_: e.matmul(
                        pb[sb_][:, k * 128:(k + 1) * 128],
                        k32[32 * g:32 * g + 16, c, tj * 128:(tj + 1) * 128],
                        q32[32 * g:32 * g + 16, c, ti * 128:(ti + 1) * 128],
                        start=True, stop=True, tile_position=(32 * g, 0)),
                        inc="tn" if k == len(tjs) - 1 else None)
                return P.counts["tn"]

            def emit_wave_poly_av(ti, h, w, qk_done, nb_):
                tjs = list(range(4 * w, min(4 * w + 4, ti + 1)))
                sb_ = w % 2
                P.wait("scalar", "tn", qk_done)
                for k, tj in enumerate(tjs):
                    aslot = (w % 2) * 4 + k
                    P.op("scalar", lambda e, k=k, sb_=sb_, aslot=aslot: e.activation(
                        aa8[:, aslot, :], pb[sb_][:, k * 128:(k + 1) * 128], AF.Square,
                        bias=bconst, scale=QPOLY_SCALE),
                        inc="sl")
                bank_consumed(sb_, "sl")
                P.wait("vector", "sl")
                for k, tj in enumerate(tjs):
                    aslot = (w % 2) * 4 + k
                    if tj == ti:
                        P.op("vector", lambda e, aslot=aslot: e.scalar_tensor_tensor(
                            aa8[:, aslot, :], aa8[:, aslot, :], 0.5, triu_s,
                            op0=ALU.add, op1=ALU.mult),
                            inc="ve" if k == len(tjs) - 1 else None)
                    else:
                        P.op("vector", lambda e, aslot=aslot: e.tensor_scalar(
                            aa8[:, aslot, :], aa8[:, aslot, :], 0.5, None, op0=ALU.add),
                            inc="ve" if k == len(tjs) - 1 else None)
                P.wait("tensor", "ve")
                for k, tj in enumerate(tjs):
                    aslot = (w % 2) * 4 + k
                    P.op("tensor", lambda e, tj=tj, ti=ti, h=h, nb_=nb_, aslot=aslot: e.matmul(
                        pb[nb_][:, 0:65], aa8[:, aslot, :],
                        v65[:, tj, h * 65:(h + 1) * 65],
                        start=(tj == 0), stop=(tj == ti)),
                        inc="tn" if tj == ti else None)

            for ti in range(NT):
                for h in range(8):
                    g, c = h % 4, h // 4
                    nb_ = 2 + (h % 2)
                    bank_write_ready(nb_)
                    nwv = (ti + 4) // 4
                    qk_done = emit_wave_qk(g, c, ti, 0)
                    for w in range(nwv):
                        nxt = emit_wave_qk(g, c, ti, w + 1) if w + 1 < nwv else None
                        emit_wave_poly_av(ti, h, w, qk_done, nb_)
                        qk_done = nxt
                    P.wait("vector", "tn")
                    P.op("vector", lambda e, nb_=nb_: e.tensor_scalar(
                        sm(DEN), pb[nb_][:, 64:65], 1e-12, None, op0=ALU.add), inc="ve")
                    P.wait("vector", "ve")
                    P.op("vector", lambda e: e.reciprocal(sm(RCP), sm(DEN)), inc="ve")
                    P.wait("vector", "ve")
                    P.op("vector", lambda e, nb_=nb_, h=h: e.tensor_scalar(
                        ystage[:, h * 64:(h + 1) * 64], pb[nb_][:, 0:64], sm(RCP), None,
                        op0=ALU.mult), inc="ve")
                    bank_consumed(nb_, "ve")
                P.wait("tensor", "ve")
                transpose_to(ystage[:], 4, yT, 0)
                bank_write_ready(6)
                bank_write_ready(7)
                P.wait("tensor", "sl")
                for kc in range(4):
                    P.op("tensor", lambda e, kc=kc: e.matmul(
                        pb[6][:, :], yT[:, kc, :], wob[:, kc, 0:512],
                        start=(kc == 0), stop=(kc == 3)), inc="tn" if kc == 3 else None)
                for kc in range(4):
                    P.op("tensor", lambda e, kc=kc: e.matmul(
                        pb[7][:, :], yT[:, kc, :], wob[:, kc, 512:1024],
                        start=(kc == 0), stop=(kc == 3)), inc="tn" if kc == 3 else None)
                if hp == 0:
                    P.wait("vector", "tn")
                    P.wait("vector", "gd")
                    P.op("vector", lambda e: e.tensor_copy(zpstage[:, 0:512], pb[6][:, :]), inc="ve")
                    bank_consumed(6, "ve")
                    P.op("vector", lambda e: e.tensor_copy(zpstage[:, 512:1024], pb[7][:, :]), inc="ve")
                    bank_consumed(7, "ve")
                    P.wait("gpsimd", "ve")
                    P.op("gpsimd", lambda e, ti=ti: e.dma_start(
                        out=zp3[ti], in_=zpstage), inc="gd", n=16)
                else:
                    P.wait("sync", "ve")
                    P.op("sync", lambda e, ti=ti: e.dma_start(
                        out=zpstage, in_=zp3[ti]), inc="ld", n=16)
                    P.wait("vector", "tn")
                    P.wait("vector", "ld")
                    P.op("vector", lambda e, ti=ti: e.tensor_tensor(
                        z[:, ti, 0:512], z[:, ti, 0:512], pb[6][:, :], op=ALU.add), inc="ve")
                    bank_consumed(6, "ve")
                    P.op("vector", lambda e, ti=ti: e.tensor_tensor(
                        z[:, ti, 512:1024], z[:, ti, 512:1024], pb[7][:, :], op=ALU.add), inc="ve")
                    bank_consumed(7, "ve")
                    P.op("vector", lambda e, ti=ti: e.tensor_tensor(
                        z[:, ti, :], z[:, ti, :], zpstage, op=ALU.add), inc="ve")

    def dump_z(seq):
        P.barrier(engines=("gpsimd",))
        for t in range(NT):
            P.op("gpsimd", lambda e, seq=seq, t=t: e.dma_start(
                out=out3[seq * NT + t], in_=z[:, t, :]), inc="gd", n=16)

    def dump_xnq():
        P.barrier(engines=("gpsimd",))
        av = arenaA[:, 0:KC * T].rearrange("p (c t) -> p c t", c=NT)
        for i in range(NT):
            P.op("gpsimd", lambda e, i=i: e.dma_start(
                out=out3[i], in_=av[:, i, :]), inc="gd", n=16)

    def dump_wgtq(w_dram):
        P.barrier(engines=("gpsimd",))
        w3v = w_dram.rearrange("(c p) n -> c p n", p=128)
        nch = w3v.shape[0]
        ww = w3v.shape[2]
        ndump = NT * n_seq
        njp = (ww + D - 1) // D
        for c in range(nch):
            if (c + 1) * njp > ndump:
                break
            for j in range((ww + D - 1) // D):
                wj = min(D, ww - j * D)
                P.op("gpsimd", lambda e, c=c, j=j, wj=wj, w3v=w3v, njp=njp: e.dma_start(
                    out=out3[c * njp + j][:, 0:wj], in_=w3v[c, :, j * D:j * D + wj]), inc="gd", n=16)

    for seq in range(n_seq):
        P.barrier()
        if "f1" in phases:
            ffn(seq, first=True)
        else:
            for t in range(NT):
                P.op("sync", lambda e, seq=seq, t=t: e.dma_start(
                    out=z[:, t, :], in_=x3[seq * NT + t]), inc="ld", n=16)
            P.barrier()
        if debug_out == "z1":
            dump_z(seq)
            continue
        if debug_out in ("xnq", "wgtq", "wdtq") and seq == 0:
            if debug_out == "xnq":
                dump_xnq()
            elif debug_out == "wgtq":
                dump_wgtq(wgTq_d)
            else:
                dump_wgtq(wdTq_d)
            break
        if "attn" in phases:
            attention(seq)
        if debug_out == "z2":
            dump_z(seq)
            continue
        P.barrier()
        if "f2" in phases:
            ffn(seq, first=False)
        dump_z(seq)

    P.wait("gpsimd", "gd")

    with nc.Block() as block:
        P.materialize(block)
    stack.close()
    return nc


def kernel(L0_attn_hidden_states, L0_attn_Wq, L0_attn_Wk, L0_attn_Wv, L0_attn_Wo,
           L0_ffn_w_gate, L0_ffn_w_up, L0_ffn_w_down,
           _phases=("f1", "attn", "f2"), _debug_out=None):
    x = np.ascontiguousarray(np.asarray(L0_attn_hidden_states, dtype=np.float32))
    B, L, Dm = x.shape
    n_cores = 8
    spc = B // n_cores
    common = {
        "wqT": np.ascontiguousarray(np.asarray(L0_attn_Wq, np.float32).T),
        "wkT": np.ascontiguousarray(np.asarray(L0_attn_Wk, np.float32).T),
        "wvT": np.ascontiguousarray(np.asarray(L0_attn_Wv, np.float32).T),
        "woT": np.ascontiguousarray(np.asarray(L0_attn_Wo, np.float32).T),
        "wg": np.ascontiguousarray(np.asarray(L0_ffn_w_gate, np.float32)),
        "wu": np.ascontiguousarray(np.asarray(L0_ffn_w_up, np.float32)),
        "wd": np.ascontiguousarray(np.asarray(L0_ffn_w_down, np.float32)),
        "ident": np.eye(128, dtype=np.float32),
        "triu": np.triu(np.ones((128, 128), dtype=np.float32)),
    }
    nc = build_core_program(n_seq=spc, phases=_phases, debug_out=_debug_out)
    in_maps = []
    for c in range(n_cores):
        m = dict(common)
        m["x"] = np.ascontiguousarray(x[c * spc:(c + 1) * spc].reshape(spc * L, Dm))
        in_maps.append(m)
    res = run_bass_kernel_spmd(nc, in_maps, core_ids=list(range(n_cores)))
    outs = [np.asarray(res.results[c]["out"]).reshape(spc, L, Dm) for c in range(n_cores)]
    return np.concatenate(outs, axis=0)
